# version 05
# baseline (speedup 1.0000x reference)
import os
import numpy as np
import ml_dtypes
import concourse.bass as bass
import concourse.mybir as mybir
from concourse.bass_utils import run_bass_kernel_spmd

F32 = mybir.dt.float32
BF16 = mybir.dt.bfloat16
AF = mybir.ActivationFunctionType
ALU = mybir.AluOpType
AX = mybir.AxisListType

D = 2048
TCORE = 1024
NDT = 16
NTT = 8
E8 = 8
WIN = 640
NCORES = 8
KVHF = 262144

ENGS = ("sync", "gpsimd", "tensor", "vector", "scalar")
NSUB = int(os.environ.get("KMK_NSUB", "6"))
FFNSTAGE = int(os.environ.get("KMK_FFNSTAGE", "4"))
TOPK = int(os.environ.get("KMK_TOPK", "5"))


class B:
    def __init__(self, nc):
        self.nc = nc
        self.ops = {e: [] for e in ENGS}
        self.semh = {}
        self.cnt = {}
        self.wait_cache = {e: {} for e in ENGS}

    def mksem(self, name, handle):
        self.semh[name] = handle
        self.cnt[name] = 0

    def emit(self, eng, fn, inc=None, incn=1):
        if inc is not None:
            self.cnt[inc] += incn
            h = self.semh[inc]
            self.ops[eng].append(lambda E, fn=fn, h=h, n=incn: fn(E).then_inc(h, n))
            return self.cnt[inc]
        self.ops[eng].append(fn)
        return None

    def wait(self, eng, sem, val):
        if not val or val <= 0:
            return
        wc = self.wait_cache[eng]
        if wc.get(sem, 0) >= val:
            return
        wc[sem] = val
        h = self.semh[sem]
        self.ops[eng].append(lambda E, h=h, v=val: E.wait_ge(h, v))

    def barrier(self, engines=("tensor", "vector", "scalar")):
        base = self.cnt["bar"]
        h = self.semh["bar"]
        for e in engines:
            self.cnt["bar"] += 1
            self.ops[e].append(lambda E, h=h: E.sem_inc(h, 1))
        tgt = base + len(engines)
        for e in engines:
            self.wait(e, "bar", tgt)

    def flush(self):
        nc, ops = self.nc, self.ops
        with nc.Block() as blk:
            @blk.sync
            def _(E):
                for f in ops["sync"]:
                    f(E)

            @blk.gpsimd
            def _(E):
                for f in ops["gpsimd"]:
                    f(E)

            @blk.tensor
            def _(E):
                for f in ops["tensor"]:
                    f(E)

            @blk.vector
            def _(E):
                for f in ops["vector"]:
                    f(E)

            @blk.scalar
            def _(E):
                for f in ops["scalar"]:
                    f(E)
        self.ops = {e: [] for e in ENGS}
        self.wait_cache = {e: {} for e in ENGS}


def dap(t, off, pairs):
    return bass.AP(t, off, [list(p) for p in pairs])


class WRing:
    def __init__(self, b, tens, ncols):
        self.b, self.t, self.n = b, tens, ncols
        self.head = 0
        self.live = []

    def load(self, ncols, dram_t, off, in_pairs):
        b = self.b
        if self.head + ncols > self.n:
            self.head = 0
        start, end = self.head, self.head + ncols
        self.head = end
        need, keep = 0, []
        for (s0, e0, rel) in self.live:
            if s0 < end and e0 > start:
                need = max(need, rel)
            else:
                keep.append((s0, e0, rel))
        self.live = keep
        if need:
            b.wait("gpsimd", "mm", need)
        row = self.t.shape[1]
        rem = [p[1] for p in in_pairs[1:]]
        strides, acc = [], 1
        for cnt in reversed(rem):
            strides.append(acc)
            acc *= cnt
        strides = list(reversed(strides))
        out_pairs = [[row, 128]] + [[s, c] for c, s in zip(rem, strides)]
        out = bass.AP(self.t, start, out_pairs)
        v = b.emit("gpsimd", lambda E, o=out, i=dap(dram_t, off, in_pairs): E.dma_start(o, i),
                   inc="dsg", incn=16)
        return start, v

    def mark(self, start, ncols, rel):
        self.live.append((start, start + ncols, rel))


def build_program():
    nc = bass.Bass("TRN2", target_bir_lowering=False, detect_race_conditions=False)

    def din(name, shape, dt=F32):
        return nc.dram_tensor(name, list(shape), dt, kind="ExternalInput")

    x_fm = din("x_fm", (D, TCORE))
    out_t = nc.dram_tensor("out", [D, TCORE], F32, kind="ExternalOutput")
    gwT = din("gwT", (D, E8))
    wg_t = din("wg", (E8, D, 1024))
    wu_t = din("wu", (E8, D, 1024))
    wd_t = din("wd", (E8, 1024, D))
    sg_t = din("sg", (D, 2048))
    su_t = din("su", (D, 2048))
    sd_t = din("sd", (2048, D))
    CwT = din("CwT", (D, D))
    DwT = din("DwT", (D, D))
    Cb_t = din("Cb", (D,))
    Db_t = din("Db", (D,))
    ld_t = din("log_delta", (D,))
    A_t = din("A", (D, 64))
    Bm_t = din("Bm", (D, 64))
    WqT_t = din("WqT", (D, D))
    WkT_t = din("WkT", (D, 512))
    WvT_t = din("WvT", (D, 512))
    WoT_t = din("WoT", (D, D))
    qnw_t = din("qnw", (128,))
    knw_t = din("knw", (128,))
    g1T = din("g1T", (D, E8))
    g2T = din("g2T", (E8, E8))
    wupT_t = din("wupT", (E8, D, 1024))
    bup_t = din("bup", (E8, 1024))
    wdnT_t = din("wdnT", (E8, 1024, D))
    bdn_t = din("bdn", (E8, D))
    cosk_t = din("cosk", (128, 1536))
    sink_t = din("sink", (128, 1536))
    mask_t = din("mask01", (5, 128, WIN))
    valid_t = din("valid01", (128,))
    ident_t = din("ident", (128, 128))
    rotm_t = din("rotm", (128, 128))

    cc_scan_in = nc.dram_tensor("cc_scan_in", [D], F32, kind="Internal")
    cc_scan_out = nc.dram_tensor("cc_scan_out", [NCORES * D], F32, kind="Internal")
    cc_kv_in = nc.dram_tensor("cc_kv_in", [2 * KVHF], BF16, kind="Internal")
    cc_kv_out = nc.dram_tensor("cc_kv_out", [NCORES * 2 * KVHF], BF16, kind="Internal")

    al = nc.alloc_sbuf_tensor
    alat = nc.alloc_sbuf_tensor_at
    xres = al("xres", [128, NDT * TCORE], F32)
    xn = al("xn", [128, NDT * TCORE], BF16)
    scr16 = al("scr16", [128, 32768], BF16)
    SCR = nc.lookup_mloc(scr16).addr
    scr32 = alat("scr32", [128, 16384], F32, offset=SCR)
    wring_t = al("wring", [128, 6144], BF16)
    U16 = al("union16", [128, 12288], BF16)
    U = nc.lookup_mloc(U16).addr
    xb = alat("xb", [128, 1024], BF16, offset=U + 0)
    x2b = alat("x2b", [128, 1024], BF16, offset=U + 2048)
    nmrow = alat("nmrow", [1, 1024], F32, offset=U + 4096)
    rs_bf = alat("rs_bf", [1, 1024], BF16, offset=U + 8192)
    nb_bf = alat("nb_bf", [1, 1024], BF16, offset=U + 10240)
    rsb = alat("rsb", [128, 1024], BF16, offset=U + 12288)
    nbb = alat("nbb", [128, 1024], BF16, offset=U + 14336)
    lntmp = alat("lntmp", [128, 512], BF16, offset=U + 16384)
    mrow = alat("mrow", [1, 1024], F32, offset=U + 17408)
    hg_sh = alat("hg_sh", [128, 8192], BF16, offset=U + 0)
    sigo = alat("sigo", [128, 512], F32, offset=U + 22528)
    sigo2 = alat("sigo2", [128, 512], F32, offset=U + 16384)
    cwbt = alat("cwbt", [128, 1024], BF16, offset=U + 18432)
    cwbt2 = alat("cwbt2", [128, 1024], BF16, offset=U + 20480)
    scanbuf = alat("scanbuf", [128, 1024], F32, offset=U + 0)
    sbdn = alat("sbdn", [8, 2048], BF16, offset=U + 0)
    umask = alat("umask", [128, 3200], BF16, offset=U + 0)
    ucos = alat("ucos", [128, 1536], BF16, offset=U + 6400)
    usin = alat("usin", [128, 1536], BF16, offset=U + 9472)
    qnb = alat("qnb", [128, 512], F32, offset=U + 12544)
    qrt = alat("qrt", [128, 512], F32, offset=U + 14592)
    rsbq = alat("rsbq", [128, 512], F32, offset=U + 16640)
    rrow = alat("rrow", [1, 512], F32, offset=U + 18688)
    q2s = alat("q2s", [128, 512], BF16, offset=U + 20736)
    rrow_bf = alat("rrow_bf", [1, 512], BF16, offset=U + 21760)
    ones_bf = al("ones_bf", [128, 1], BF16)
    onesr_bf = al("onesr_bf", [1, 128], BF16)
    sident = al("sident", [128, 128], BF16)
    sgw = al("sgw", [128, NDT * E8], BF16)
    sg1 = al("sg1", [128, NDT * E8], BF16)
    sg2 = al("sg2", [8, 8], BF16)
    sbup = al("sbup", [128, 64], F32)
    sCb = al("sCb", [128, 16], F32)
    sDb = al("sDb", [128, 16], F32)
    sld = al("sld", [128, 16], F32)
    sABc = al("sABc", [128, 16], F32)
    sqnw = al("sqnw", [128, 1], F32)
    sknw = al("sknw", [128, 1], F32)
    w1q = al("w1q", [128, 1], F32)
    w1k = al("w1k", [128, 1], F32)
    svalid = al("svalid", [128, 1], F32)
    sAc = al("sAc", [128, 16], F32)
    sA1024 = al("sA1024", [128, 16], F32)
    lends = al("lends", [128, 128], F32)
    sE = al("sE", [128, 96], F32)
    sG = al("sG", [128, 16], F32)
    szero16 = al("szero16", [128, 16], F32)
    lendl = al("lendl", [128, 16], F32)
    cwf = al("cwf", [128, 64], F32)
    cwb16 = al("cwb16", [128, 64], BF16)
    cwT = al("cwT", [8, 1024], BF16)
    thbf = al("thbf", [128, 64], BF16)
    gcol = al("gcol", [128, 4], F32)
    rcol = al("rcol", [128, 8], F32)
    attT_sb = al("attT_sb", [128, 640], BF16)
    cwrow = al("cwrow", [1, 2048], BF16)
    lgt = al("lgt", [128, 64], F32)
    gq = al("gq", [128, 32], F32)
    sdiag = al("sdiag", [128, 128], BF16)
    srot = al("srot", [128, 128], BF16)

    QF0 = 0
    KE0 = 16384
    VE0 = 22528
    EXA = 14336
    EXB = 14848
    AT0 = 29952
    AB0 = 30464

    def hgs_ap(e, itg, tc):
        col = ((e * 4 + (itg % 4)) * 2 + tc) * 512
        return scr16[:, col:col + 512]

    def kext_ap(kvh, col, n=512):
        c = KE0 + kvh * 1536 + col
        return scr16[:, c:c + n]

    psum_cm, Pt = [], []
    for i in range(7):
        cm = nc.psum_tensor(f"P{i}", [128, 512], F32)
        psum_cm.append(cm)
        Pt.append(cm.__enter__())

    b = B(nc)
    sem_cm = []
    for s in ("dsg", "dss", "mm", "ev", "sc", "ccs", "bar"):
        cm = nc.semaphore(s)
        sem_cm.append(cm)
        b.mksem(s, cm.__enter__())
    wring = WRing(b, wring_t, 6144)

    preq = {i: [] for i in range(7)}

    def use_slot(i):
        for s, v in preq[i]:
            b.wait("tensor", s, v)
        preq[i] = []

    def bc_ap(tens, col, n):
        return bass.AP(tens, col, [[tens.shape[1], 128], [0, n]])

    def init_loads():
        b.emit("sync", lambda E: E.dma_start(
            bass.AP(xres, 0, [[NDT * TCORE, 128], [TCORE, NDT], [1, TCORE]]),
            dap(x_fm, 0, [[TCORE, 128], [128 * TCORE, NDT], [1, TCORE]])), inc="dss", incn=16)
        for dst, src, pairs in [
                (sCb, Cb_t, [[1, 128], [128, 16]]),
                (sDb, Db_t, [[1, 128], [128, 16]]),
                (sld, ld_t, [[1, 128], [128, 16]]),
                (sqnw, qnw_t, [[1, 128]]),
                (sknw, knw_t, [[1, 128]]),
                (svalid, valid_t, [[1, 128]]),
                (sbup, bup_t, [[1, 128], [1024, 8], [128, 8]])]:
            if len(pairs) == 1:
                o = bass.AP(dst, 0, [[dst.shape[1], 128], [1, 1]])
            else:
                rem = [p[1] for p in pairs[1:]]
                strides, acc = [], 1
                for cnt in reversed(rem):
                    strides.append(acc)
                    acc *= cnt
                strides = list(reversed(strides))
                o = bass.AP(dst, 0, [[dst.shape[1], 128]] + [[s, c] for c, s in zip(rem, strides)])
            b.emit("sync", lambda E, o=o, i=dap(src, 0, pairs): E.dma_start(
                o, i, allow_slow_non_contiguous=True), inc="dss", incn=16)
        b.emit("sync", lambda E: E.dma_start(
            bass.AP(scr32, 0, [[16384, 128], [64, 16], [1, 64]]),
            dap(A_t, 0, [[64, 128], [128 * 64, 16], [1, 64]])), inc="dss", incn=16)
        b.emit("sync", lambda E: E.dma_start(
            bass.AP(scr32, 1024, [[16384, 128], [64, 16], [1, 64]]),
            dap(Bm_t, 0, [[64, 128], [128 * 64, 16], [1, 64]])), inc="dss", incn=16)
        dv = b.cnt["dss"]
        b.emit("gpsimd", lambda E: E.dma_start(
            bass.AP(sgw, 0, [[NDT * E8, 128], [E8, NDT], [1, E8]]),
            dap(gwT, 0, [[E8, 128], [128 * E8, NDT], [1, E8]])), inc="dsg", incn=16)
        b.emit("gpsimd", lambda E: E.dma_start(
            bass.AP(sg1, 0, [[NDT * E8, 128], [E8, NDT], [1, E8]]),
            dap(g1T, 0, [[E8, 128], [128 * E8, NDT], [1, E8]])), inc="dsg", incn=16)
        b.emit("gpsimd", lambda E: E.dma_start(sg2[:, :], g2T.ap()), inc="dsg", incn=16)
        b.emit("gpsimd", lambda E: E.dma_start(sident[:, :], ident_t.ap()), inc="dsg", incn=16)
        b.emit("gpsimd", lambda E: E.dma_start(srot[:, :], rotm_t.ap()), inc="dsg", incn=16)
        gv = b.cnt["dsg"]
        b.wait("vector", "dss", dv)
        b.wait("vector", "dsg", gv)
        b.emit("vector", lambda E: E.memset(szero16[:, :], 0.0))
        b.emit("vector", lambda E: E.memset(ones_bf[:, :], 1.0))
        b.emit("vector", lambda E: E.memset(onesr_bf[:, :], 1.0))
        for dt in range(NDT):
            b.emit("vector", lambda E, dt=dt: E.tensor_mul(
                scr32[:, dt * 64:(dt + 1) * 64], scr32[:, dt * 64:(dt + 1) * 64],
                scr32[:, 1024 + dt * 64:1024 + (dt + 1) * 64]),
                inc=("ev" if dt == NDT - 1 else None))
        b.wait("vector", "ev", b.cnt["ev"])
        for dt in range(NDT):
            b.emit("vector", lambda E, dt=dt: E.reduce_sum(
                sABc[:, dt:dt + 1], scr32[:, dt * 64:(dt + 1) * 64], axis=AX.X))
        b.emit("vector", lambda E: E.tensor_scalar(sA1024[:, :], sld[:, :], -1.0, None,
                                                   op0=ALU.mult))
        b.emit("vector", lambda E: E.tensor_scalar(w1q[:, :], sqnw[:, :], 1.0,
                                                   float(128 ** -0.5), op0=ALU.add,
                                                   op1=ALU.mult))
        b.emit("vector", lambda E: E.tensor_scalar(w1k[:, :], sknw[:, :], 1.0, None,
                                                   op0=ALU.add), inc="ev")
        evz = b.cnt["ev"]
        b.wait("scalar", "dss", dv)
        b.wait("scalar", "ev", evz)
        b.emit("scalar", lambda E: E.activation(sAc[:, :], sA1024[:, :], AF.Sigmoid),
               inc="sc")
        scv = b.cnt["sc"]
        b.wait("vector", "sc", scv)
        b.emit("vector", lambda E: E.tensor_copy(sA1024[:, :], sAc[:, :]), inc="ev")
        b.wait("vector", "ev", b.cnt["ev"])
        for _ in range(10):
            b.emit("vector", lambda E: E.tensor_mul(sA1024[:, :], sA1024[:, :],
                                                    sA1024[:, :]), inc="ev")
            b.wait("vector", "ev", b.cnt["ev"])
        b.emit("vector", lambda E: E.engine_nop(), inc="ev")

    def layernorm():
        statmm = {}
        sqb = [(lntmp[:, :], q2s[:, :]), (rsb[:, 0:512], rsb[:, 512:1024])]
        for dt in range(NDT):
            if dt >= 2:
                b.wait("scalar", "mm", statmm[dt - 2])
            xbs = xb if dt % 2 == 0 else x2b
            sq0, sq1 = sqb[dt % 2]
            b.emit("scalar", lambda E, dt=dt, xbs=xbs: E.activation(
                xbs[:, :], xres[:, dt * TCORE:(dt + 1) * TCORE], AF.Copy))
            b.emit("scalar", lambda E, dt=dt, sq0=sq0: E.activation(
                sq0, xres[:, dt * TCORE: dt * TCORE + 512], AF.Square))
            b.emit("scalar", lambda E, dt=dt, sq1=sq1: E.activation(
                sq1, xres[:, dt * TCORE + 512: dt * TCORE + 1024], AF.Square), inc="sc")
            scv = b.cnt["sc"]
            b.wait("tensor", "sc", scv)
            if dt == 0:
                for i in range(4):
                    use_slot(i)
            last = None
            for i, (src, c0) in enumerate([(xbs[:, 0:512], None), (xbs[:, 512:1024], None),
                                           (sq0, None), (sq1, None)]):
                last = b.emit("tensor", lambda E, i=i, src=src, dt=dt: E.matmul(
                    Pt[i][0:1, 0:512], ones_bf[:, :], src,
                    start=(dt == 0), stop=(dt == NDT - 1)), inc="mm")
            statmm[dt] = last
        vstat = statmm[NDT - 1]
        b.wait("vector", "mm", vstat)
        for tc in range(2):
            b.emit("vector", lambda E, tc=tc: E.tensor_scalar(
                nmrow[0:1, tc * 512:(tc + 1) * 512], Pt[tc][0:1, 0:512], -1.0 / 2048.0, None,
                op0=ALU.mult))
            b.emit("vector", lambda E, tc=tc: E.tensor_scalar(
                mrow[0:1, tc * 512:(tc + 1) * 512], Pt[2 + tc][0:1, 0:512], 1.0 / 2048.0,
                None, op0=ALU.mult))
        b.emit("vector", lambda E: E.tensor_mul(rs_bf[0:1, :], nmrow[0:1, :], nmrow[0:1, :]))
        b.emit("vector", lambda E: E.tensor_sub(mrow[0:1, :], mrow[0:1, :], rs_bf[0:1, :]))
        b.emit("vector", lambda E: E.tensor_scalar(mrow[0:1, :], mrow[0:1, :], 1e-5, None,
                                                   op0=ALU.add), inc="ev")
        vv = b.cnt["ev"]
        b.wait("scalar", "ev", vv)
        b.emit("scalar", lambda E: E.activation(mrow[0:1, :], mrow[0:1, :], AF.Sqrt),
               inc="sc")
        scv = b.cnt["sc"]
        b.wait("vector", "sc", scv)
        b.emit("vector", lambda E: E.reciprocal(mrow[0:1, :], mrow[0:1, :]))
        b.emit("vector", lambda E: E.tensor_copy(rs_bf[0:1, :], mrow[0:1, :]))
        b.emit("vector", lambda E: E.tensor_mul(mrow[0:1, :], mrow[0:1, :], nmrow[0:1, :]))
        b.emit("vector", lambda E: E.tensor_copy(nb_bf[0:1, :], mrow[0:1, :]), inc="ev")
        vv2 = b.cnt["ev"]
        b.wait("tensor", "ev", vv2)
        use_slot(4)
        use_slot(5)
        bc_done = 0
        for tc in range(2):
            if tc == 1:
                b.wait("tensor", "ev", bc_done)
            b.emit("tensor", lambda E, tc=tc: E.matmul(
                Pt[4][:, 0:512], onesr_bf[:, :], rs_bf[0:1, tc * 512:(tc + 1) * 512],
                start=True, stop=True))
            v1 = b.emit("tensor", lambda E, tc=tc: E.matmul(
                Pt[5][:, 0:512], onesr_bf[:, :], nb_bf[0:1, tc * 512:(tc + 1) * 512],
                start=True, stop=True), inc="mm")
            b.wait("vector", "mm", v1)
            b.emit("vector", lambda E, tc=tc: E.tensor_copy(rsb[:, tc * 512:(tc + 1) * 512],
                                                            Pt[4][:, 0:512]))
            bc_done = b.emit("vector", lambda E, tc=tc: E.tensor_copy(
                nbb[:, tc * 512:(tc + 1) * 512], Pt[5][:, 0:512]), inc="ev")
        pv = b.cnt["ev"]
        for i in range(6):
            preq[i] = [("ev", pv)]
        apv = None
        for dt in range(NDT):
            for tc in range(2):
                b.emit("vector", lambda E, dt=dt, tc=tc: E.tensor_mul(
                    lntmp[:, :], xres[:, dt * TCORE + tc * 512: dt * TCORE + (tc + 1) * 512],
                    rsb[:, tc * 512:(tc + 1) * 512]))
                apv = b.emit("vector", lambda E, dt=dt, tc=tc: E.tensor_add(
                    xn[:, dt * TCORE + tc * 512: dt * TCORE + (tc + 1) * 512],
                    lntmp[:, :], nbb[:, tc * 512:(tc + 1) * 512]), inc="ev")
        return apv

    def vfence():
        b.wait("vector", "ev", b.cnt["ev"])

    def topk_from_logits():
        vv0 = b.cnt["ev"]
        b.wait("vector", "ev", vv0)
        b.wait("scalar", "ev", vv0)
        for tt in range(NTT):
            b.emit("vector", lambda E, tt=tt: E.reduce_max(
                gq[:, tt:tt + 1], lgt[:, tt * 8:(tt + 1) * 8], axis=AX.X, negate=True),
                inc=("ev" if tt == NTT - 1 else None))
        vfence()
        b.emit("vector", lambda E: E.tensor_scalar(gq[:, 8:16], gq[:, 0:8], -1.0, None,
                                                   op0=ALU.mult), inc="ev")
        vfence()
        for tt in range(NTT):
            b.emit("vector", lambda E, tt=tt: E.tensor_tensor(
                cwf[:, tt * 8:(tt + 1) * 8], lgt[:, tt * 8:(tt + 1) * 8],
                bc_ap(gq, 8 + tt, 8), ALU.is_equal),
                inc=("ev" if tt == NTT - 1 else None))
        vfence()
        b.emit("vector", lambda E: E.scalar_tensor_tensor(
            cwf[:, :], cwf[:, :], -1e30, lgt[:, :], op0=ALU.mult, op1=ALU.add), inc="ev")
        vfence()
        for tt in range(NTT):
            b.emit("vector", lambda E, tt=tt: E.reduce_max(
                gq[:, 16 + tt:17 + tt], cwf[:, tt * 8:(tt + 1) * 8], axis=AX.X),
                inc=("ev" if tt == NTT - 1 else None))
        vfence()
        for tt in range(NTT):
            b.emit("vector", lambda E, tt=tt: E.tensor_tensor(
                cwf[:, tt * 8:(tt + 1) * 8], lgt[:, tt * 8:(tt + 1) * 8],
                bc_ap(gq, 16 + tt, 8), ALU.is_ge))
        b.emit("vector", lambda E: E.tensor_add(gq[:, 24:32], gq[:, 16:24], gq[:, 0:8]),
               inc="ev")
        vv = b.cnt["ev"]
        b.wait("scalar", "ev", vv)
        for tt in range(NTT):
            b.emit("scalar", lambda E, tt=tt: E.activation(
                sigo[:, tt * 8:(tt + 1) * 8], lgt[:, tt * 8:(tt + 1) * 8], AF.Exp,
                bias=gq[:, tt:tt + 1]))
        b.emit("scalar", lambda E: E.activation(gq[:, 24:32], gq[:, 24:32], AF.Exp),
               inc="sc")
        scv = b.cnt["sc"]
        b.wait("vector", "sc", scv)
        b.emit("vector", lambda E: E.tensor_scalar(gq[:, 24:32], gq[:, 24:32], 1.0, None,
                                                   op0=ALU.add), inc="ev")
        vfence()
        b.emit("vector", lambda E: E.reciprocal(gq[:, 24:32], gq[:, 24:32]), inc="ev")
        vfence()
        b.emit("vector", lambda E: E.tensor_mul(cwf[:, :], cwf[:, :], sigo[:, 0:64]),
               inc="ev")
        vfence()
        for tt in range(NTT):
            b.emit("vector", lambda E, tt=tt: E.tensor_tensor(
                cwb16[:, tt * 8:(tt + 1) * 8], cwf[:, tt * 8:(tt + 1) * 8],
                bc_ap(gq, 24 + tt, 8), ALU.mult), inc="ev")
        vcw = b.cnt["ev"]
        b.wait("tensor", "ev", vcw)
        use_slot(3)
        use_slot(6)
        for tt in range(NTT):
            ps = Pt[3] if tt % 2 == 0 else Pt[6]
            if tt >= 2:
                b.wait("tensor", "ev", b.cnt["ev"] - 1)
            v = b.emit("tensor", lambda E, tt=tt, ps=ps: E.matmul(
                ps[0:8, 0:128], cwb16[:, tt * 8:(tt + 1) * 8], sident[:, :],
                start=True, stop=True), inc="mm")
            b.wait("vector", "mm", v)
            b.emit("vector", lambda E, tt=tt, ps=ps: E.tensor_copy(
                cwT[0:8, tt * 128:(tt + 1) * 128], ps[0:8, 0:128]), inc="ev")
        preq[3] = [("ev", b.cnt["ev"])]
        preq[6] = [("ev", b.cnt["ev"])]
        return b.cnt["ev"]

    def gate_logits(wbuf, xn_ev):
        b.wait("tensor", "ev", xn_ev)
        for tt in range(NTT):
            use_slot(6)
            vmm = None
            for dt in range(NDT):
                vmm = b.emit("tensor", lambda E, tt=tt, dt=dt, wbuf=wbuf: E.matmul(
                    Pt[6][0:128, 0:8],
                    xn[:, dt * TCORE + tt * 128: dt * TCORE + tt * 128 + 128],
                    wbuf[:, dt * 8:(dt + 1) * 8], start=(dt == 0), stop=(dt == NDT - 1)),
                    inc=("mm" if dt == NDT - 1 else None))
            b.wait("vector", "mm", vmm)
            lv = b.emit("vector", lambda E, tt=tt: E.tensor_copy(
                lgt[:, tt * 8:(tt + 1) * 8], Pt[6][0:128, 0:8]), inc="ev")
            preq[6] = [("ev", lv)]

    def gate_l0(xn_ev):
        gate_logits(sgw, xn_ev)
        if FFNSTAGE <= 0:
            return b.cnt["ev"]
        return topk_from_logits()

    def gate_l1(xn_ev):
        gate_logits(sg1, xn_ev)
        vv0 = b.cnt["ev"]
        b.wait("scalar", "ev", vv0)
        for tt in range(NTT):
            b.emit("scalar", lambda E, tt=tt: E.activation(
                thbf[:, tt * 8:(tt + 1) * 8], lgt[:, tt * 8:(tt + 1) * 8], AF.Tanh),
                inc="sc")
        scv = b.cnt["sc"]
        b.wait("tensor", "sc", scv)
        use_slot(3)
        use_slot(6)
        thT_done = None
        for tt in range(NTT):
            ps = Pt[3] if tt % 2 == 0 else Pt[6]
            if tt >= 2:
                b.wait("tensor", "ev", b.cnt["ev"] - 1)
            v = b.emit("tensor", lambda E, tt=tt, ps=ps: E.matmul(
                ps[0:8, 0:128], thbf[:, tt * 8:(tt + 1) * 8], sident[:, :],
                start=True, stop=True), inc="mm")
            b.wait("vector", "mm", v)
            thT_done = b.emit("vector", lambda E, tt=tt, ps=ps: E.tensor_copy(
                cwT[0:8, tt * 128:(tt + 1) * 128], ps[0:8, 0:128]), inc="ev")
        preq[3] = [("ev", thT_done)]
        preq[6] = [("ev", thT_done)]
        b.wait("tensor", "ev", thT_done)
        for tt in range(NTT):
            use_slot(6)
            v = b.emit("tensor", lambda E, tt=tt: E.matmul(
                Pt[6][0:128, 0:8], cwT[0:8, tt * 128:(tt + 1) * 128],
                sg2[:, :], start=True, stop=True), inc="mm")
            b.wait("vector", "mm", v)
            lv = b.emit("vector", lambda E, tt=tt: E.tensor_copy(
                lgt[:, tt * 8:(tt + 1) * 8], Pt[6][0:128, 0:8]), inc="ev")
            preq[6] = [("ev", lv)]
        return topk_from_logits()

    def ffn(layer, xn_ev):
        if layer == 0:
            cw_ev = gate_l0(xn_ev)
            sbdn_v = None
        else:
            cw_ev = gate_l1(xn_ev)
            b.wait("gpsimd", "ev", xn_ev)
            b.emit("gpsimd", lambda E: E.dma_start(sbdn[:, :], bdn_t.ap()), inc="dsg", incn=16)
            sbdn_v = b.cnt["dsg"]
        if FFNSTAGE <= 1:
            return b.cnt["ev"]
        wgT_d = wg_t if layer == 0 else wupT_t
        wdT_d = wd_t if layer == 0 else wdnT_t
        cwslots = (cwbt, cwbt2)
        rowgate = {}
        for ih in range(2):
            cwb_ev = {}
            for e in range(E8):
                slot = cwslots[e % 2]
                rs0 = (e % 2) * 1024
                b.wait("sync", "ev", cw_ev)
                prevg = rowgate.get((ih, e - 2)) or rowgate.get((ih - 1, e + 6))
                if prevg:
                    b.wait("sync", "mm", prevg)
                rowv = b.emit("sync", lambda E, e=e, rs0=rs0: E.dma_start(
                    cwrow[0:1, rs0:rs0 + 1024], cwT[e:e + 1, :]), inc="dss", incn=16)
                b.wait("tensor", "dss", rowv)
                use_slot(6)
                for tc in range(2):
                    v = b.emit("tensor", lambda E, rs0=rs0, tc=tc: E.matmul(
                        Pt[6][:, 0:512], onesr_bf[:, :],
                        cwrow[0:1, rs0 + tc * 512: rs0 + (tc + 1) * 512],
                        start=True, stop=True), inc="mm")
                    b.wait("vector", "mm", v)
                    b.emit("vector", lambda E, slot=slot, tc=tc: E.tensor_copy(
                        slot[:, tc * 512:(tc + 1) * 512], Pt[6][:, 0:512]), inc="ev")
                    if tc == 0:
                        b.wait("tensor", "ev", b.cnt["ev"])
                preq[6] = [("ev", b.cnt["ev"])]
                cwb_ev[e] = b.cnt["ev"]
                rowgate[(ih, e)] = b.cnt["mm"]
                for it in range(4):
                    itg = ih * 4 + it
                    srcs = [wgT_d] if layer == 1 else [wgT_d, wu_t]
                    chunks = []
                    for src in srcs:
                        off = e * D * 1024 + ih * 512 + it * 128
                        st, dv = wring.load(2048, src, off,
                                            [[1024, 128], [128 * 1024, NDT], [1, 128]])
                        chunks.append((st, dv))
                    for tc in range(2):
                        gi = (it * 2 + tc) % 2
                        b.wait("tensor", "dsg", max(dv for _, dv in chunks))
                        b.wait("tensor", "ev", xn_ev)
                        pslots = [gi] if layer == 1 else [gi, 2 + gi]
                        for ps in pslots:
                            use_slot(ps)
                        vmm = None
                        for (st, _), ps in zip(chunks, pslots):
                            for dt in range(NDT):
                                vmm = b.emit("tensor",
                                             lambda E, st=st, dt=dt, tc=tc, ps=ps: E.matmul(
                                                 Pt[ps][:, 0:512],
                                                 wring_t[:, st + dt * 128: st + (dt + 1) * 128],
                                                 xn[:, dt * TCORE + tc * 512:
                                                    dt * TCORE + (tc + 1) * 512],
                                                 start=(dt == 0), stop=(dt == NDT - 1)),
                                             inc=("mm" if dt == NDT - 1 else None))
                        if tc == 1:
                            for (st, _) in chunks:
                                wring.mark(st, 2048, vmm)
                        b.wait("scalar", "mm", vmm)
                        if layer == 0:
                            b.emit("scalar", lambda E, gi=gi: E.activation(
                                sigo[:, :], Pt[gi][:, 0:512], AF.Silu), inc="sc")
                        else:
                            b.emit("scalar", lambda E, gi=gi, e=e, itg=itg: E.activation(
                                sigo[:, :], Pt[gi][:, 0:512], AF.Silu,
                                bias=sbup[:, e * 8 + itg: e * 8 + itg + 1]), inc="sc")
                        scv = b.cnt["sc"]
                        b.wait("vector", "sc", scv)
                        b.wait("vector", "ev", cwb_ev[e])
                        dst = hgs_ap(e, itg, tc)
                        if layer == 0:
                            b.emit("vector", lambda E, gi=gi: E.tensor_mul(
                                sigo2[:, :], sigo[:, :], Pt[2 + gi][:, 0:512]))
                            hv = b.emit("vector", lambda E, dst=dst, slot=slot, tc=tc:
                                        E.tensor_mul(dst, sigo2[:, :],
                                                     slot[:, tc * 512:(tc + 1) * 512]),
                                        inc="ev")
                        else:
                            hv = b.emit("vector", lambda E, dst=dst, slot=slot, tc=tc:
                                        E.tensor_mul(dst, sigo[:, :],
                                                     slot[:, tc * 512:(tc + 1) * 512]),
                                        inc="ev")
                        for ps in pslots:
                            preq[ps] = [("ev", hv)]
            if FFNSTAGE <= 2:
                continue
            if layer == 0:
                for ist in range(8):
                    isg = ih * 8 + ist
                    st_g, dv_g = wring.load(2048, sg_t, isg * 128,
                                            [[2048, 128], [128 * 2048, NDT], [1, 128]])
                    st_u, dv_u = wring.load(2048, su_t, isg * 128,
                                            [[2048, 128], [128 * 2048, NDT], [1, 128]])
                    for tc in range(2):
                        gi = (ist * 2 + tc) % 2
                        b.wait("tensor", "dsg", max(dv_g, dv_u))
                        b.wait("tensor", "ev", xn_ev)
                        use_slot(gi)
                        use_slot(2 + gi)
                        vmm = None
                        for st, ps in ((st_g, gi), (st_u, 2 + gi)):
                            for dt in range(NDT):
                                vmm = b.emit("tensor",
                                             lambda E, st=st, dt=dt, tc=tc, ps=ps: E.matmul(
                                                 Pt[ps][:, 0:512],
                                                 wring_t[:, st + dt * 128: st + (dt + 1) * 128],
                                                 xn[:, dt * TCORE + tc * 512:
                                                    dt * TCORE + (tc + 1) * 512],
                                                 start=(dt == 0), stop=(dt == NDT - 1)),
                                             inc=("mm" if dt == NDT - 1 else None))
                        if tc == 1:
                            wring.mark(st_g, 2048, vmm)
                            wring.mark(st_u, 2048, vmm)
                        b.wait("scalar", "mm", vmm)
                        b.emit("scalar", lambda E, gi=gi: E.activation(
                            sigo[:, :], Pt[gi][:, 0:512], AF.Silu), inc="sc")
                        scv = b.cnt["sc"]
                        b.wait("vector", "sc", scv)
                        dst = hg_sh[:, (ist * 2 + tc) * 512:(ist * 2 + tc + 1) * 512]
                        hv = b.emit("vector", lambda E, gi=gi, dst=dst: E.tensor_mul(
                            dst, sigo[:, :], Pt[2 + gi][:, 0:512]), inc="ev")
                        preq[gi] = [("ev", hv)]
                        preq[2 + gi] = [("ev", hv)]
            hgs_ev = b.cnt["ev"]
            if FFNSTAGE <= 3:
                continue
            for ht in range(NDT):
                echunks = []
                for e in range(E8):
                    off = e * 1024 * D + (ih * 512) * D + ht * 128
                    st, dv = wring.load(512, wdT_d, off, [[D, 128], [128 * D, 4], [1, 128]])
                    echunks.append((st, dv))
                if layer == 0:
                    st_sd, dv_sd = wring.load(1024, sd_t, (ih * 1024) * D + ht * 128,
                                              [[D, 128], [128 * D, 8], [1, 128]])
                for tc in range(2):
                    ps = 4 + tc
                    use_slot(ps)
                    b.wait("tensor", "ev", hgs_ev)
                    b.wait("tensor", "dsg", max(dv for _, dv in echunks))
                    if layer == 0:
                        b.wait("tensor", "dsg", dv_sd)
                    mms = []
                    for e in range(E8):
                        st = echunks[e][0]
                        for kt in range(4):
                            itg = ih * 4 + kt
                            mms.append(lambda E, start, stop, st=st, kt=kt, e=e, itg=itg,
                                       tc=tc, ps=ps: E.matmul(
                                Pt[ps][:, 0:512],
                                wring_t[:, st + kt * 128: st + (kt + 1) * 128],
                                hgs_ap(e, itg, tc), start=start, stop=stop))
                    if layer == 0:
                        for kt in range(8):
                            mms.append(lambda E, start, stop, st=st_sd, kt=kt, tc=tc, ps=ps:
                                       E.matmul(
                                Pt[ps][:, 0:512],
                                wring_t[:, st + kt * 128: st + (kt + 1) * 128],
                                hg_sh[:, (kt * 2 + tc) * 512:(kt * 2 + tc + 1) * 512],
                                start=start, stop=stop))
                    elif ih == 0:
                        b.wait("tensor", "dsg", sbdn_v)
                        mms.append(lambda E, start, stop, ht=ht, tc=tc, ps=ps: E.matmul(
                            Pt[ps][:, 0:512], sbdn[0:8, ht * 128:(ht + 1) * 128],
                            cwT[0:8, tc * 512:(tc + 1) * 512], start=start, stop=stop))
                    n = len(mms)
                    vmm = None
                    for j, f in enumerate(mms):
                        vmm = b.emit("tensor", lambda E, f=f, j=j, n=n: f(
                            E, start=(j == 0), stop=(j == n - 1)),
                            inc=("mm" if j == n - 1 else None))
                    if tc == 1:
                        for e in range(E8):
                            wring.mark(echunks[e][0], 512, vmm)
                        if layer == 0:
                            wring.mark(st_sd, 1024, vmm)
                    b.wait("vector", "mm", vmm)
                    dst = xres[:, ht * TCORE + tc * 512: ht * TCORE + (tc + 1) * 512]
                    dv_ = b.emit("vector", lambda E, dst=dst, ps=ps: E.scalar_tensor_tensor(
                        dst, Pt[ps][:, 0:512], 0.5, dst, op0=ALU.mult, op1=ALU.add), inc="ev")
                    preq[ps] = [("ev", dv_)]
        return b.cnt["ev"]

    def ssm(xn_ev):
        for ot in range(NDT):
            st_c, dv_c = wring.load(2048, CwT, ot * 128, [[D, 128], [128 * D, NDT], [1, 128]])
            st_d, dv_d = wring.load(2048, DwT, ot * 128, [[D, 128], [128 * D, NDT], [1, 128]])
            for tc in range(2):
                gi = tc
                b.wait("tensor", "dsg", max(dv_c, dv_d))
                b.wait("tensor", "ev", xn_ev)
                use_slot(gi)
                use_slot(2 + gi)
                vmm = None
                for st, ps in ((st_c, gi), (st_d, 2 + gi)):
                    for dt in range(NDT):
                        vmm = b.emit("tensor", lambda E, st=st, dt=dt, tc=tc, ps=ps: E.matmul(
                            Pt[ps][:, 0:512],
                            wring_t[:, st + dt * 128: st + (dt + 1) * 128],
                            xn[:, dt * TCORE + tc * 512: dt * TCORE + (tc + 1) * 512],
                            start=(dt == 0), stop=(dt == NDT - 1)),
                            inc=("mm" if dt == NDT - 1 else None))
                if tc == 1:
                    wring.mark(st_c, 2048, vmm)
                    wring.mark(st_d, 2048, vmm)
                b.wait("scalar", "mm", vmm)
                b.emit("scalar", lambda E, gi=gi, ot=ot: E.activation(
                    sigo[:, :], Pt[2 + gi][:, 0:512], AF.Sigmoid, bias=sDb[:, ot:ot + 1]),
                    inc="sc")
                scv = b.cnt["sc"]
                b.wait("vector", "sc", scv)
                dst = scr32[:, ot * TCORE + tc * 512: ot * TCORE + (tc + 1) * 512]
                xv = b.emit("vector", lambda E, gi=gi, ot=ot, dst=dst: E.scalar_tensor_tensor(
                    dst, Pt[gi][:, 0:512], sCb[:, ot:ot + 1], sigo[:, :],
                    op0=ALU.add, op1=ALU.mult), inc="ev")
                preq[gi] = [("ev", xv)]
                preq[2 + gi] = [("ev", xv)]
            b.emit("vector", lambda E, ot=ot: E.tensor_tensor_scan(
                scanbuf[:, :], bc_ap(sAc, ot, TCORE), scr32[:, ot * TCORE:(ot + 1) * TCORE],
                0.0, op0=ALU.mult, op1=ALU.add), inc="ev")
            vfence()
            b.emit("vector", lambda E, ot=ot: E.tensor_copy(lendl[:, ot:ot + 1],
                                                            scanbuf[:, 1023:1024]), inc="ev")
        lend_ev = b.cnt["ev"]
        b.wait("sync", "ev", lend_ev)
        b.emit("sync", lambda E: E.dma_start(
            dap(cc_scan_in, 0, [[1, 128], [128, 16]]), lendl[:, :],
            allow_slow_non_contiguous=True), inc="dss", incn=16)
        dsv = b.cnt["dss"]
        b.wait("gpsimd", "dss", dsv)
        b.emit("gpsimd", lambda E: E.collective_compute(
            "AllGather", ALU.bypass, replica_groups=[list(range(NCORES))],
            ins=[cc_scan_in.ap().opt()], outs=[cc_scan_out.ap().opt()]), inc="ccs")
        ccv = b.cnt["ccs"]
        b.wait("sync", "ccs", ccv)
        b.emit("sync", lambda E: E.dma_start(
            bass.AP(lends, 0, [[128, 128], [16, 8], [1, 16]]),
            dap(cc_scan_out, 0, [[1, 128], [2048, 8], [128, 16]]),
            allow_slow_non_contiguous=True), inc="dss", incn=16)
        dsv = b.cnt["dss"]
        b.wait("vector", "dss", dsv)
        for g, rbase in ((0, 0), (1, 4)):
            b.emit("vector", lambda E, g=g, rbase=rbase: E.tensor_copy(
                sE[:, g * 48: g * 48 + 16], lends[:, rbase * 16:(rbase + 1) * 16]),
                inc="ev")
            vfence()
            for j in (1, 2):
                b.emit("vector", lambda E, g=g, j=j: E.tensor_mul(
                    sE[:, g * 48 + j * 16: g * 48 + (j + 1) * 16],
                    sE[:, g * 48 + (j - 1) * 16: g * 48 + j * 16], sA1024[:, :]), inc="ev")
                vfence()
                b.emit("vector", lambda E, g=g, j=j, rbase=rbase: E.tensor_add(
                    sE[:, g * 48 + j * 16: g * 48 + (j + 1) * 16],
                    sE[:, g * 48 + j * 16: g * 48 + (j + 1) * 16],
                    lends[:, (rbase + j) * 16:(rbase + j + 1) * 16]), inc="ev")
                vfence()
        b.emit("vector", lambda E: E.engine_nop(), inc="ev")

    def ssm_finish():
        for dt in range(NDT):
            b.emit("vector", lambda E, dt=dt: E.tensor_tensor_scan(
                scr32[:, dt * TCORE:(dt + 1) * TCORE], bc_ap(sAc, dt, TCORE),
                scr32[:, dt * TCORE:(dt + 1) * TCORE], sG[:, dt:dt + 1],
                op0=ALU.mult, op1=ALU.add))
            b.emit("vector", lambda E, dt=dt: E.scalar_tensor_tensor(
                xres[:, dt * TCORE:(dt + 1) * TCORE], scr32[:, dt * TCORE:(dt + 1) * TCORE],
                sABc[:, dt:dt + 1], xres[:, dt * TCORE:(dt + 1) * TCORE],
                op0=ALU.mult, op1=ALU.add), inc="ev")

    def normrope(ps, w1col, ccol, dst_ap):
        vmm0 = b.cnt["mm"]
        b.wait("scalar", "mm", vmm0)
        b.emit("scalar", lambda E, ps=ps: E.activation(q2s[:, :], Pt[ps][:, 0:512], AF.Square),
               inc="sc")
        scv = b.cnt["sc"]
        b.wait("tensor", "sc", scv)
        use_slot(6)
        v1 = b.emit("tensor", lambda E: E.matmul(Pt[6][0:1, 0:512], ones_bf[:, :], q2s[:, :],
                                                 start=True, stop=True), inc="mm")
        b.wait("vector", "mm", v1)
        b.emit("vector", lambda E: E.tensor_scalar(rrow[0:1, :], Pt[6][0:1, 0:512],
                                                   1.0 / 128.0, 1e-6, op0=ALU.mult,
                                                   op1=ALU.add), inc="ev")
        vv = b.cnt["ev"]
        b.wait("scalar", "ev", vv)
        b.emit("scalar", lambda E: E.activation(rrow[0:1, :], rrow[0:1, :], AF.Sqrt), inc="sc")
        scv = b.cnt["sc"]
        b.wait("vector", "sc", scv)
        b.emit("vector", lambda E: E.reciprocal(rrow[0:1, :], rrow[0:1, :]))
        b.emit("vector", lambda E: E.tensor_copy(rrow_bf[0:1, :], rrow[0:1, :]), inc="ev")
        vv = b.cnt["ev"]
        b.wait("tensor", "ev", vv)
        v2 = b.emit("tensor", lambda E: E.matmul(Pt[6][:, 0:512], onesr_bf[:, :],
                                                 rrow_bf[0:1, :], start=True, stop=True),
                    inc="mm")
        b.wait("vector", "mm", v2)
        rv = b.emit("vector", lambda E: E.tensor_copy(rsbq[:, :], Pt[6][:, 0:512]), inc="ev")
        preq[6] = [("ev", rv)]
        vq = b.emit("vector", lambda E, ps=ps, w1col=w1col: E.scalar_tensor_tensor(
            qnb[:, :], Pt[ps][:, 0:512], w1col, rsbq[:, :], op0=ALU.mult, op1=ALU.mult),
            inc="ev")
        preq[ps] = [("ev", vq)]
        b.wait("scalar", "ev", vq)
        scq = b.emit("scalar", lambda E: E.activation(q2s[:, :], qnb[:, :], AF.Copy),
                     inc="sc")
        b.wait("tensor", "sc", scq)
        use_slot(6)
        vr = b.emit("tensor", lambda E: E.matmul(Pt[6][:, 0:512], srot[:, :], q2s[:, :],
                                                 start=True, stop=True), inc="mm")
        b.emit("vector", lambda E, ccol=ccol: E.tensor_mul(qrt[:, :], qnb[:, :],
                                                           ucos[:, ccol:ccol + 512]))
        b.wait("vector", "mm", vr)
        vrot = b.emit("vector", lambda E, ccol=ccol: E.tensor_mul(
            rsbq[:, :], Pt[6][:, 0:512], usin[:, ccol:ccol + 512]), inc="ev")
        preq[6] = [("ev", vrot)]
        b.emit("vector", lambda E, dst_ap=dst_ap: E.tensor_add(dst_ap, qrt[:, :], rsbq[:, :]),
               inc="ev")
        return b.cnt["ev"]

    def attention(xn_ev):
        b.wait("gpsimd", "ev", xn_ev)
        b.emit("gpsimd", lambda E: E.dma_start(
            bass.AP(umask, 0, [[3200, 128], [640, 5], [1, 640]]),
            dap(mask_t, 0, [[WIN, 128], [128 * WIN, 5], [1, WIN]])), inc="dsg", incn=16)
        b.emit("gpsimd", lambda E: E.dma_start(ucos[:, :], cosk_t.ap()), inc="dsg", incn=16)
        b.emit("gpsimd", lambda E: E.dma_start(usin[:, :], sink_t.ap()), inc="dsg", incn=16)
        s5v = b.cnt["dsg"]
        b.wait("vector", "dsg", s5v)

        for tc in range(2):
            for dt in range(NDT):
                st, dv = wring.load(512, WkT_t, dt * 128 * 512, [[512, 128], [1, 512]])
                b.wait("tensor", "dsg", dv)
                b.wait("tensor", "ev", xn_ev)
                if dt == 0:
                    for i in range(4):
                        use_slot(i)
                vmm = None
                for kvh in range(4):
                    vmm = b.emit("tensor", lambda E, st=st, kvh=kvh, dt=dt, tc=tc: E.matmul(
                        Pt[kvh][:, 0:512], wring_t[:, st + kvh * 128: st + (kvh + 1) * 128],
                        xn[:, dt * TCORE + tc * 512: dt * TCORE + (tc + 1) * 512],
                        start=(dt == 0), stop=(dt == NDT - 1)),
                        inc=("mm" if kvh == 3 else None))
                wring.mark(st, 512, vmm)
            for kvh in range(4):
                normrope(kvh, w1k[:, 0:1], 512 + tc * 512, kext_ap(kvh, 512 + tc * 512))
        for half in range(2):
            for dt in range(NDT):
                st, dv = wring.load(512, WvT_t, dt * 128 * 512, [[512, 128], [1, 512]])
                b.wait("tensor", "dsg", dv)
                b.wait("tensor", "ev", xn_ev)
                if dt == 0:
                    for i in range(4):
                        use_slot(i)
                vmm = None
                for i in range(4):
                    tt = half * 4 + i
                    vmm = b.emit("tensor", lambda E, st=st, i=i, tt=tt, dt=dt: E.matmul(
                        Pt[i][:, 0:512],
                        xn[:, dt * TCORE + tt * 128: dt * TCORE + tt * 128 + 128],
                        wring_t[:, st: st + 512],
                        start=(dt == 0), stop=(dt == NDT - 1)),
                        inc=("mm" if i == 3 else None))
                wring.mark(st, 512, vmm)
            b.wait("vector", "mm", b.cnt["mm"])
            for i in range(4):
                tt = half * 4 + i
                vv = b.emit("vector", lambda E, i=i, tt=tt: E.tensor_copy(
                    scr16[:, VE0 + (4 + tt) * 512: VE0 + (5 + tt) * 512], Pt[i][:, 0:512]),
                    inc="ev")
                preq[i] = [("ev", vv)]
        kvdone = b.cnt["ev"]

        b.wait("sync", "ev", kvdone)
        b.emit("sync", lambda E: E.dma_start(
            dap(cc_kv_in, 0, [[512, 128], [65536, 4], [1, 512]]),
            bass.AP(scr16, 17408, [[32768, 128], [1536, 4], [1, 512]])), inc="dss", incn=16)
        b.emit("sync", lambda E: E.dma_start(
            dap(cc_kv_in, 262144, [[2048, 128], [1, 2048]]),
            bass.AP(scr16, 26624, [[32768, 128], [1, 2048]])), inc="dss", incn=16)
        dsv = b.cnt["dss"]
        b.wait("gpsimd", "dss", dsv)
        b.emit("gpsimd", lambda E: E.collective_compute(
            "AllGather", ALU.bypass, replica_groups=[list(range(NCORES))],
            ins=[cc_kv_in.ap().opt()], outs=[cc_kv_out.ap().opt()]), inc="ccs")
        ccv = b.cnt["ccs"]

        for tc in range(2):
            for hq in range(4):
                for dt in range(NDT):
                    st, dv = wring.load(512, WqT_t, dt * 128 * D + hq * 512,
                                        [[D, 128], [1, 512]])
                    b.wait("tensor", "dsg", dv)
                    if dt == 0:
                        for i in range(4):
                            use_slot(i)
                    vmm = None
                    for i in range(4):
                        vmm = b.emit("tensor", lambda E, st=st, i=i, dt=dt, tc=tc: E.matmul(
                            Pt[i][:, 0:512], wring_t[:, st + i * 128: st + (i + 1) * 128],
                            xn[:, dt * TCORE + tc * 512: dt * TCORE + (tc + 1) * 512],
                            start=(dt == 0), stop=(dt == NDT - 1)),
                            inc=("mm" if i == 3 else None))
                    wring.mark(st, 512, vmm)
                for i in range(4):
                    h = hq * 4 + i
                    dst = scr16[:, QF0 + h * 1024 + tc * 512:
                                QF0 + h * 1024 + (tc + 1) * 512]
                    normrope(i, w1q[:, 0:1], 512 + tc * 512, dst)
        b.wait("sync", "ccs", ccv)
        return ccv

    def attention_main():
        for kvh in range(4):
            b.emit("vector", lambda E, kvh=kvh: E.tensor_scalar(
                kext_ap(kvh, 0), kext_ap(kvh, 0), svalid[:, 0:1], None, op0=ALU.mult))
        for st in range(4):
            b.emit("vector", lambda E, st=st: E.tensor_scalar(
                scr16[:, VE0 + st * 512: VE0 + (st + 1) * 512],
                scr16[:, VE0 + st * 512: VE0 + (st + 1) * 512],
                svalid[:, 0:1], None, op0=ALU.mult))
        b.emit("vector", lambda E: E.engine_nop(), inc="ev")
        halo_ev = b.cnt["ev"]
        b.wait("tensor", "ev", halo_ev)

        expA = scr32[:, EXA:EXA + 512]
        expB = scr32[:, EXB:EXB + 128]
        attA = scr16[:, AT0:AT0 + 512]
        attB = scr16[:, AB0:AB0 + 128]
        prev_attT_mm = 0
        prev_ctx_mm = 0
        for g in range(2):
            for tloc in range(4):
                tt = g * 4 + tloc
                midx = (1 + tt) if tt < 4 else 0
                maskA = umask[:, midx * 640: midx * 640 + 512]
                maskB = umask[:, midx * 640 + 512: midx * 640 + 640]
                for h in range(16):
                    kvh = h // 4
                    qsl = scr16[:, QF0 + h * 1024 + tt * 128: QF0 + h * 1024 + tt * 128 + 128]
                    use_slot(0)
                    use_slot(1)
                    b.emit("tensor", lambda E, qsl=qsl, kvh=kvh, tt=tt: E.matmul(
                        Pt[0][:, 0:512], qsl, kext_ap(kvh, tt * 128), start=True, stop=True))
                    smm = b.emit("tensor", lambda E, qsl=qsl, kvh=kvh, tt=tt: E.matmul(
                        Pt[1][:, 0:128], qsl, kext_ap(kvh, tt * 128 + 512, 128),
                        start=True, stop=True), inc="mm")
                    b.wait("vector", "mm", smm)
                    b.emit("vector", lambda E: E.reduce_max(rcol[:, 0:1], Pt[0][:, 0:512],
                                                            axis=AX.X, negate=True))
                    b.emit("vector", lambda E: E.reduce_max(rcol[:, 1:2], Pt[1][:, 0:128],
                                                            axis=AX.X, negate=True), inc="ev")
                    vfence()
                    b.emit("vector", lambda E: E.tensor_tensor(
                        rcol[:, 2:3], rcol[:, 0:1], rcol[:, 1:2], ALU.min), inc="ev")
                    vv = b.cnt["ev"]
                    b.wait("scalar", "ev", vv)
                    b.emit("scalar", lambda E, expA=expA: E.activation(
                        expA, Pt[0][:, 0:512], AF.Exp, bias=rcol[:, 2:3]))
                    sce = b.emit("scalar", lambda E, expB=expB: E.activation(
                        expB, Pt[1][:, 0:128], AF.Exp, bias=rcol[:, 2:3]), inc="sc")
                    preq[0] = [("sc", sce)]
                    preq[1] = [("sc", sce)]
                    b.wait("vector", "sc", sce)
                    b.wait("vector", "mm", prev_attT_mm)
                    b.emit("vector", lambda E, attA=attA, maskA=maskA, expA=expA:
                           E.scalar_tensor_tensor(attA, expA, 1.0, maskA, op0=ALU.mult,
                                                  op1=ALU.mult, accum_out=rcol[:, 3:4]))
                    b.emit("vector", lambda E, attB=attB, maskB=maskB, expB=expB:
                           E.scalar_tensor_tensor(attB, expB, 1.0, maskB, op0=ALU.mult,
                                                  op1=ALU.mult, accum_out=rcol[:, 4:5]),
                           inc="ev")
                    vfence()
                    b.emit("vector", lambda E: E.tensor_add(rcol[:, 5:6], rcol[:, 3:4],
                                                            rcol[:, 4:5]), inc="ev")
                    vfence()
                    b.emit("vector", lambda E: E.reciprocal(rcol[:, 5:6], rcol[:, 5:6]),
                           inc="ev")
                    vfence()
                    b.emit("vector", lambda E: E.tensor_tensor(
                        sdiag[:, :], sident[:, :], bc_ap(rcol, 5, 128), ALU.mult),
                        inc="ev")
                    dgv = b.cnt["ev"]
                    b.wait("tensor", "ev", dgv)
                    use_slot(3)
                    use_slot(6)
                    for j in range(4):
                        b.emit("tensor", lambda E, j=j, attA=attA: E.matmul(
                            Pt[3][:, j * 128:(j + 1) * 128], attA[:, j * 128:(j + 1) * 128],
                            sdiag[:, :], start=True, stop=True))
                    prev_attT_mm = b.emit("tensor", lambda E, attB=attB: E.matmul(
                        Pt[6][:, 0:128], attB, sdiag[:, :], start=True, stop=True), inc="mm")
                    b.wait("scalar", "mm", prev_attT_mm)
                    b.wait("scalar", "mm", prev_ctx_mm)
                    b.emit("scalar", lambda E: E.activation(attT_sb[:, 0:512], Pt[3][:, 0:512],
                                                            AF.Copy))
                    scv = b.emit("scalar", lambda E: E.activation(
                        attT_sb[:, 512:640], Pt[6][:, 0:128], AF.Copy), inc="sc")
                    preq[3] = [("sc", scv)]
                    preq[6] = [("sc", scv)]
                    b.wait("tensor", "sc", scv)
                    use_slot(2)
                    for j in range(5):
                        stv = tt + j
                        prev_ctx_mm = b.emit("tensor",
                                             lambda E, stv=stv, kvh=kvh, j=j: E.matmul(
                                                 Pt[2][:, 0:128],
                                                 scr16[:, VE0 + stv * 512 + kvh * 128:
                                                       VE0 + stv * 512 + kvh * 128 + 128],
                                                 attT_sb[:, j * 128:(j + 1) * 128],
                                                 start=(j == 0), stop=(j == 4)),
                                             inc=("mm" if j == 4 else None))
                    b.wait("vector", "mm", prev_ctx_mm)
                    cev = b.emit("vector", lambda E, h=h, tt=tt: E.tensor_copy(
                        scr16[:, QF0 + h * 1024 + tt * 128: QF0 + h * 1024 + tt * 128 + 128],
                        Pt[2][:, 0:128]), inc="ev")
                    preq[2] = [("ev", cev)]
            ctx_ev = b.cnt["ev"]
            for ot in range(NDT):
                st, dv = wring.load(2048, WoT_t, ot * 128, [[D, 128], [128 * D, NDT], [1, 128]])
                ps = 4 + (ot % 2)
                use_slot(ps)
                b.wait("tensor", "dsg", dv)
                b.wait("tensor", "ev", ctx_ev)
                vmm = None
                for hh in range(16):
                    vmm = b.emit("tensor", lambda E, st=st, hh=hh, g=g, ps=ps: E.matmul(
                        Pt[ps][:, 0:512], wring_t[:, st + hh * 128: st + (hh + 1) * 128],
                        scr16[:, QF0 + hh * 1024 + g * 512: QF0 + hh * 1024 + (g + 1) * 512],
                        start=(hh == 0), stop=(hh == 15)),
                        inc=("mm" if hh == 15 else None))
                wring.mark(st, 2048, vmm)
                b.wait("vector", "mm", vmm)
                dst = xres[:, ot * TCORE + g * 512: ot * TCORE + (g + 1) * 512]
                dv_ = b.emit("vector", lambda E, dst=dst, ps=ps: E.scalar_tensor_tensor(
                    dst, Pt[ps][:, 0:512], 1.0, dst, op0=ALU.mult, op1=ALU.add), inc="ev")
                preq[ps] = [("ev", dv_)]

    init_loads()
    b.wait("vector", "dss", 16)
    b.wait("scalar", "dss", 16)

    xn_ev = layernorm()
    if NSUB >= 1:
        ffn(0, xn_ev)
    if NSUB == 0:
        pass
    if NSUB >= 2:
        b.barrier()
        xn_ev = layernorm()
        ssm(xn_ev)
        b.flush()
        for arm in nc.switch_core_id(engines=[nc.vector], n=128):
            ci = arm.logical % 8
            q = ci % 4
            grp = ci // 4
            if q == 0:
                nc.vector.tensor_copy(sG[:, :], szero16[:, :]).then_inc(b.semh["ev"], 1)
            else:
                nc.vector.tensor_copy(sG[:, :], sE[:, grp * 48 + (q - 1) * 16:
                                                   grp * 48 + q * 16]).then_inc(
                    b.semh["ev"], 1)
        b.cnt["ev"] += 1
        b.wait("vector", "ev", b.cnt["ev"])
        ssm_finish()
    if NSUB >= 3:
        b.barrier()
        xn_ev = layernorm()
        ffn(0, xn_ev)
    if NSUB >= 4:
        b.barrier()
        xn_ev = layernorm()
        ffn(1, xn_ev)
    if NSUB >= 5:
        b.barrier()
        xn_ev = layernorm()
        attention(xn_ev)
        b.flush()
        for arm in nc.switch_core_id(engines=[nc.sync], n=128):
            ci = arm.logical % 8
            pred = (ci - 1) % 8
            nc.sync.dma_start(
                bass.AP(scr16, 16384, [[32768, 128], [1536, 4], [1, 512]]),
                dap(cc_kv_out, pred * 2 * KVHF, [[512, 128], [65536, 4], [1, 512]])
            ).then_inc(b.semh["dss"], 16)
            nc.sync.dma_start(
                bass.AP(scr16, 22528, [[32768, 128], [1, 2048]]),
                dap(cc_kv_out, pred * 2 * KVHF + 262144, [[2048, 128], [1, 2048]])
            ).then_inc(b.semh["dss"], 16)
        b.cnt["dss"] += 32
        b.wait("vector", "dss", b.cnt["dss"])
        attention_main()
    if NSUB >= 6:
        b.barrier()
        xn_ev = layernorm()
        ffn(1, xn_ev)

    b.barrier()
    vout = b.cnt["ev"]
    b.wait("sync", "ev", vout)
    b.emit("sync", lambda E: E.dma_start(
        dap(out_t, 0, [[TCORE, 128], [128 * TCORE, NDT], [1, TCORE]]),
        bass.AP(xres, 0, [[NDT * TCORE, 128], [TCORE, NDT], [1, TCORE]])), inc="dss", incn=16)
    if os.environ.get("KMK_DUMP"):
        dxn = nc.dram_tensor("d_xn", [128, NDT * TCORE], BF16, kind="ExternalOutput")
        dlg = nc.dram_tensor("d_lgt", [128, 64], F32, kind="ExternalOutput")
        dcw = nc.dram_tensor("d_cwf", [128, 64], F32, kind="ExternalOutput")
        dct = nc.dram_tensor("d_cwT", [8, 1024], BF16, kind="ExternalOutput")
        dhg = nc.dram_tensor("d_hgs", [128, 512], BF16, kind="ExternalOutput")
        dcb = nc.dram_tensor("d_cwb", [128, 1024], BF16, kind="ExternalOutput")
        dcr = nc.dram_tensor("d_cwrow", [1, 2048], BF16, kind="ExternalOutput")
        for dt_, sb_ in ((dxn, xn), (dlg, lgt), (dcw, cwf), (dct, cwT),
                         (dhg, scr16[:, 32256:32768]), (dcb, cwbt2[:, :]),
                         (dcr, cwrow[:, :])):
            b.emit("sync", lambda E, dt_=dt_, sb_=sb_: E.dma_start(dt_.ap(), sb_[:, :] if hasattr(sb_, "shape") and not isinstance(sb_, bass.AP) else sb_),
                   inc="dss", incn=16)
    b.wait("sync", "dss", b.cnt["dss"])
    b.flush()
    return nc


def _mk_masks(core):
    t0 = (core % 4) * 1024
    has_pred = (core % 4) != 0
    m = np.zeros((5, 128, WIN), np.float32)
    for i in range(128):
        m[0, i, i:i + 513] = 1.0
    for tt in range(4):
        mm = m[0].copy()
        if not has_pred:
            s0 = t0 + tt * 128 - 512
            jmin = -s0
            if jmin > 0:
                mm[:, :min(jmin, WIN)] = 0.0
        m[tt + 1] = mm
    return np.ascontiguousarray(m.astype(np.float32))


def _mk_rope(core):
    t0 = (core % 4) * 1024
    pos = np.arange(t0 - 512, t0 + 1024, dtype=np.float64)
    inv = 1.0 / (10000.0 ** (np.arange(0, 128, 2, dtype=np.float64) / 128.0))
    ang = pos[None, :] * np.concatenate([inv, inv])[:, None]
    cos = np.cos(ang)
    sin_eff = np.sin(ang)
    sin_eff[:64] *= -1.0
    return (np.ascontiguousarray(cos.astype(np.float32)),
            np.ascontiguousarray(sin_eff.astype(np.float32)))


def _mk_rotm():
    r = np.zeros((128, 128), np.float32)
    r[(np.arange(128) + 64) % 128, np.arange(128)] = 1.0
    return r


def kernel(**inp):
    f32 = np.float32
    asc = np.ascontiguousarray

    def g(k):
        return np.asarray(inp[k], dtype=f32)

    shared = dict(
        gwT=asc(g("L0_ffn_gate_w").T),
        wg=asc(g("L0_ffn_wg")),
        wu=asc(g("L0_ffn_wu")),
        wd=asc(g("L0_ffn_wd")),
        sg=asc(g("L0_ffn_sg")),
        su=asc(g("L0_ffn_su")),
        sd=asc(g("L0_ffn_sd")),
        CwT=asc(g("L0_attn_C_w").T),
        DwT=asc(g("L0_attn_D_w").T),
        Cb=asc(g("L0_attn_C_b")),
        Db=asc(g("L0_attn_D_b")),
        log_delta=asc(g("L0_attn_log_delta")),
        A=asc(g("L0_attn_A")),
        Bm=asc(g("L0_attn_B")),
        WqT=asc(g("L1_attn_Wq").T),
        WkT=asc(g("L1_attn_Wk").T),
        WvT=asc(g("L1_attn_Wv").T),
        WoT=asc(g("L1_attn_Wo").T),
        qnw=asc(g("L1_attn_qn_w")),
        knw=asc(g("L1_attn_kn_w")),
        g1T=asc(g("L1_ffn_Wg1").T),
        g2T=asc(g("L1_ffn_Wg2").T),
        wupT=asc(g("L1_ffn_Wup").transpose(0, 2, 1)),
        bup=asc(g("L1_ffn_bup")),
        wdnT=asc(g("L1_ffn_Wdown").transpose(0, 2, 1)),
        bdn=asc(g("L1_ffn_bdown")),
        ident=np.eye(128, dtype=np.float32),
        rotm=_mk_rotm(),
    )
    x = g("L0_attn_x")
    in_maps = []
    for c in range(NCORES):
        bb, q = c // 4, c % 4
        m = dict(shared)
        m["x_fm"] = asc(x[bb, q * 1024:(q + 1) * 1024, :].T)
        cos, sin = _mk_rope(c)
        m["cosk"] = cos
        m["sink"] = sin
        m["mask01"] = _mk_masks(c)
        m["valid01"] = np.full(128, 1.0 if q != 0 else 0.0, f32)
        in_maps.append(m)
    nc = build_program()
    res = run_bass_kernel_spmd(nc, in_maps, core_ids=list(range(NCORES)))
    out = np.empty((2, 4096, D), f32)
    for c in range(NCORES):
        bb, q = c // 4, c % 4
        out[bb, q * 1024:(q + 1) * 1024, :] = res.results[c]["out"].T
    return out


if __name__ == "__main__":
    import reference as R
    inp = R.setup_inputs()
    got = kernel(**{k: np.asarray(v) for k, v in inp.items()})
    want = np.asarray(R.reference(**inp))
    num = float(((got - want) ** 2).mean())
    den = float((want ** 2).mean())
    print("resvar", num / den)
